# version 01
# baseline (speedup 1.0000x reference)
import numpy as np
import ml_dtypes
import concourse.bass as bass
import concourse.mybir as mybir
from concourse.bass_utils import run_bass_kernel_spmd
from contextlib import ExitStack

F32 = mybir.dt.float32
BF16 = mybir.dt.bfloat16
I32 = mybir.dt.int32
AF = mybir.ActivationFunctionType
ALU = mybir.AluOpType
AX = mybir.AxisListType

S = 4096
E = 1024
NH = 16
D = 64
I = 2816
IH = 1408
IQ = 704
TT = 32
PAIRS = 8
EC = 8

ENGINES = ("sync", "tensor", "scalar", "vector", "gpsimd")


def build_program(dbg=False):
    nc = bass.Bass("TRN2", target_bir_lowering=False)
    scratch_kind = "ExternalOutput" if dbg else "Internal"

    d_x = nc.dram_tensor("x", [S, E], F32, kind="ExternalInput")
    d_wq = nc.dram_tensor("wq", [NH, E, D], F32, kind="ExternalInput")
    d_wk = nc.dram_tensor("wk", [NH, E, D], F32, kind="ExternalInput")
    d_wv = nc.dram_tensor("wv", [NH, E, D], F32, kind="ExternalInput")
    d_wvup = nc.dram_tensor("wvup", [NH, D, E], F32, kind="ExternalInput")
    d_wg = nc.dram_tensor("wg", [I, E], F32, kind="ExternalInput")
    d_wu = nc.dram_tensor("wu", [I, E], F32, kind="ExternalInput")
    d_wd = nc.dram_tensor("wd", [E, I], F32, kind="ExternalInput")
    d_tri = nc.dram_tensor("trimask", [128, 128], F32, kind="ExternalInput")
    d_id01 = nc.dram_tensor("ident01", [128, 128], F32, kind="ExternalInput")
    d_idf = nc.dram_tensor("identf", [128, 128], F32, kind="ExternalInput")
    d_ones = nc.dram_tensor("onesbf", [1, 64], BF16, kind="ExternalInput")
    d_c3ff = nc.dram_tensor("c3ff", [128, 1], I32, kind="ExternalInput")

    d_y = nc.dram_tensor("y", [S, E], F32, kind="ExternalOutput")
    d_xn = nc.dram_tensor("xn_d", [S, E], BF16, kind=scratch_kind)
    d_xq = nc.dram_tensor("xq_d", [S, E], BF16, kind=scratch_kind)
    d_wgq = nc.dram_tensor("wgq_d", [I, E], BF16, kind=scratch_kind)
    d_wuq = nc.dram_tensor("wuq_d", [I, E], BF16, kind=scratch_kind)
    d_wdq = nc.dram_tensor("wdq_d", [E, I], BF16, kind=scratch_kind)
    d_hq = nc.dram_tensor("hq_d", [S, I], BF16, kind=scratch_kind)
    d_s = nc.dram_tensor("s_d", [S, E], BF16, kind=scratch_kind)
    d_dat = nc.dram_tensor("dat_d", [NH, S], BF16, kind=scratch_kind)
    d_dbg = None
    if dbg:
        d_dbg = nc.dram_tensor("dbg_d", [128, 2 * TT], F32, kind="ExternalOutput")

    ges = ExitStack()
    c_tri = ges.enter_context(nc.sbuf_tensor("c_tri", [128, 128], F32))
    c_id01 = ges.enter_context(nc.sbuf_tensor("c_id01", [128, 128], F32))
    c_idf = ges.enter_context(nc.sbuf_tensor("c_idf", [128, 128], F32))
    c_ones = ges.enter_context(nc.sbuf_tensor("c_ones", [1, 64], BF16))
    c_3ff = ges.enter_context(nc.sbuf_tensor("c_3ff", [128, 1], I32))
    st_rstd = ges.enter_context(nc.sbuf_tensor("st_rstd", [128, TT], F32))
    st_nmr = ges.enter_context(nc.sbuf_tensor("st_nmr", [128, TT], F32))
    lsum = ges.enter_context(nc.sbuf_tensor("lsum", [128, 1], F32))
    lsq = ges.enter_context(nc.sbuf_tensor("lsq", [128, 1], F32))
    lm = ges.enter_context(nc.sbuf_tensor("lm", [128, 1], F32))
    lm2 = ges.enter_context(nc.sbuf_tensor("lm2", [128, 1], F32))
    lv = ges.enter_context(nc.sbuf_tensor("lv", [128, 1], F32))
    lstd = ges.enter_context(nc.sbuf_tensor("lstd", [128, 1], F32))

    sems = {}
    semstack = ExitStack()
    for name in ("ld", "gp", "pe", "dve", "sca", "bar"):
        sems[name] = semstack.enter_context(nc.semaphore(name))

    counts = {k: 0 for k in sems}

    class Ctx:
        cur = None
        eng = None

    def do(eng_name, fn, inc=None, n=1, waits=()):
        if Ctx.cur == eng_name:
            for s, v in waits:
                if v > 0:
                    Ctx.eng.wait_ge(sems[s], v)
            inst = fn(Ctx.eng)
            if inc is not None:
                inst.then_inc(sems[inc], n)
        if inc is not None:
            counts[inc] += n

    def vsync():
        wait("vector", "dve", counts["dve"])

    def wait(eng_name, semname, value):
        if Ctx.cur == eng_name and value > 0:
            Ctx.eng.wait_ge(sems[semname], value)

    def barrier():
        counts["bar"] += len(ENGINES)
        tgt = counts["bar"]
        if Ctx.cur is not None:
            Ctx.eng.sem_inc(sems["bar"], 1)
            Ctx.eng.wait_ge(sems["bar"], tgt)

    def alloc_quant(ps, Fmax, tag):
        qs = {}
        qs["q1"] = ps.enter_context(nc.sbuf_tensor(f"q1{tag}", [128, Fmax], F32))
        qs["q2"] = ps.enter_context(nc.sbuf_tensor(f"q2{tag}", [128, Fmax], F32))
        qs["q3"] = ps.enter_context(nc.sbuf_tensor(f"q3{tag}", [128, Fmax], F32))
        qs["qi"] = ps.enter_context(nc.sbuf_tensor(f"qi{tag}", [128, Fmax], I32))
        qs["mi"] = ps.enter_context(nc.sbuf_tensor(f"mi{tag}", [128, Fmax], I32))
        qs["qbm"] = ps.enter_context(nc.sbuf_tensor(f"qbm{tag}", [128, Fmax // 16], F32))
        qs["qs2"] = ps.enter_context(nc.sbuf_tensor(f"qs2{tag}", [128, Fmax // 16], F32))
        qs["qr6"] = ps.enter_context(nc.sbuf_tensor(f"qr6{tag}", [128, Fmax // 16], F32))
        return qs

    def emit_quant(qs, src_ap_fn, dst_ap_fn, F, inc=None):
        q1, q2, q3, qi, mi = qs["q1"], qs["q2"], qs["q3"], qs["qi"], qs["mi"]
        qbm, qs2, qr6 = qs["qbm"], qs["qs2"], qs["qr6"]
        B = F // 16
        r3 = lambda ap: ap.rearrange("p (b k) -> p b k", k=16)
        bc = lambda t: t[:, 0:B].unsqueeze(2).broadcast_to([128, B, 16])
        V = "vector"
        do(V, lambda h: h.tensor_reduce(out=qbm[:, 0:B], in_=r3(src_ap_fn()),
                                        axis=AX.X, op=ALU.max, apply_absolute_value=True),
           inc="dve")
        vsync()
        do(V, lambda h: h.tensor_scalar(out=qs2[:, 0:B], in0=qbm[:, 0:B],
                                        scalar1=1.0 / 6.0, scalar2=1e-12,
                                        op0=ALU.mult, op1=ALU.max), inc="dve")
        vsync()
        do(V, lambda h: h.reciprocal(out=qr6[:, 0:B], in_=qs2[:, 0:B]), inc="dve")
        vsync()
        do(V, lambda h: h.tensor_tensor(out=r3(q2[:, 0:F]), in0=r3(src_ap_fn()),
                                        in1=bc(qr6), op=ALU.mult))
        do(V, lambda h: h.tensor_scalar(out=q3[:, 0:F], in0=q2[:, 0:F],
                                        scalar1=0.25, scalar2=None, op0=ALU.is_gt))
        do(V, lambda h: h.scalar_tensor_tensor(out=q1[:, 0:F], in0=q2[:, 0:F], scalar=0.75,
                                               in1=q3[:, 0:F], op0=ALU.is_gt, op1=ALU.add))
        do(V, lambda h: h.scalar_tensor_tensor(out=q3[:, 0:F], in0=q2[:, 0:F], scalar=-0.25,
                                               in1=q1[:, 0:F], op0=ALU.is_lt, op1=ALU.subtract))
        do(V, lambda h: h.scalar_tensor_tensor(out=q1[:, 0:F], in0=q2[:, 0:F], scalar=-0.75,
                                               in1=q3[:, 0:F], op0=ALU.is_lt, op1=ALU.add))
        do(V, lambda h: h.tensor_scalar(out=q3[:, 0:F], in0=q1[:, 0:F],
                                        scalar1=-0.5, scalar2=None, op0=ALU.mult))
        do(V, lambda h: h.tensor_scalar(out=qi[:, 0:F], in0=q2[:, 0:F].bitcast(I32),
                                        scalar1=21, scalar2=None, op0=ALU.logical_shift_right))
        do(V, lambda h: h.tensor_scalar(out=qi[:, 0:F], in0=qi[:, 0:F],
                                        scalar1=1.0, scalar2=None, op0=ALU.add))
        do(V, lambda h: h.tensor_scalar(out=qi[:, 0:F], in0=qi[:, 0:F],
                                        scalar1=1, scalar2=None, op0=ALU.logical_shift_right))
        do(V, lambda h: h.tensor_scalar(out=mi[:, 0:F], in0=qi[:, 0:F],
                                        scalar1=c_3ff[:, 0:1], scalar2=None, op0=ALU.bitwise_and))
        do(V, lambda h: h.tensor_scalar(out=mi[:, 0:F], in0=mi[:, 0:F],
                                        scalar1=254.0, scalar2=None, op0=ALU.is_ge))
        do(V, lambda h: h.tensor_scalar(out=q2[:, 0:F].bitcast(I32), in0=qi[:, 0:F],
                                        scalar1=22, scalar2=None, op0=ALU.logical_shift_left))
        do(V, lambda h: h.copy_predicated(out=q3[:, 0:F], mask=mi[:, 0:F], data=q2[:, 0:F]))
        do(V, lambda h: h.tensor_tensor(out=r3(dst_ap_fn()), in0=r3(q3[:, 0:F]),
                                        in1=bc(qs2), op=ALU.mult), inc=inc)

    mark = {}

    def phase_init():
        for cdst, csrc in ((c_tri, d_tri), (c_id01, d_id01), (c_idf, d_idf),
                           (c_ones, d_ones), (c_3ff, d_c3ff)):
            do("sync", lambda h, a=cdst, b=csrc: h.dma_start(out=a[:, :], in_=b[:, :]),
               inc="ld", n=16)
        mark["consts"] = counts["ld"]

    def phase0(ph):
        xt = ph["xt"]; xnf = ph["xnf"]; xnb = ph["xnb"]; xqb = ph["xqb"]
        for t in range(TT):
            p = t % 2
            do("sync", lambda h, t=t, p=p: h.dma_start(out=xt[p][:, :], in_=d_x[t * 128:(t + 1) * 128, :]),
               inc="ld", n=16,
               waits=[("sca", mark.get(("xn", t - 2), 0))])
            mark[("ldx", t)] = counts["ld"]
            wait("vector", "ld", mark[("ldx", t)])
            wait("vector", "sca", mark.get(("sqrt", t - 1), 0))
            do("vector", lambda h, p=p: h.tensor_reduce(out=lsum[:, :], in_=xt[p][:, :], axis=AX.X, op=ALU.add),
               inc="dve")
            do("vector", lambda h, p=p, qs=ph["qs"]: h.scalar_tensor_tensor(
                out=qs["q1"][:, 0:E], in0=xt[p][:, :], scalar=1.0,
                in1=xt[p][:, :], op0=ALU.mult, op1=ALU.mult,
                accum_out=lsq[:, :]), inc="dve")
            vsync()
            do("vector", lambda h: h.tensor_scalar(out=lm[:, :], in0=lsum[:, :], scalar1=1.0 / E, scalar2=None, op0=ALU.mult), inc="dve")
            vsync()
            do("vector", lambda h: h.tensor_tensor(out=lm2[:, :], in0=lm[:, :], in1=lm[:, :], op=ALU.mult), inc="dve")
            do("vector", lambda h: h.tensor_scalar(out=lv[:, :], in0=lsq[:, :], scalar1=1.0 / E, scalar2=None, op0=ALU.mult), inc="dve")
            vsync()
            do("vector", lambda h: h.tensor_tensor(out=lv[:, :], in0=lv[:, :], in1=lm2[:, :], op=ALU.subtract), inc="dve")
            vsync()
            do("vector", lambda h: h.tensor_scalar(out=lv[:, :], in0=lv[:, :], scalar1=1e-5, scalar2=None, op0=ALU.add),
               inc="dve")
            mark[("stats", t)] = counts["dve"]
            wait("scalar", "dve", mark[("stats", t)])
            do("scalar", lambda h: h.activation(lstd[:, :], lv[:, :], AF.Sqrt), inc="sca")
            mark[("sqrt", t)] = counts["sca"]
            wait("vector", "sca", mark[("sqrt", t)])
            do("vector", lambda h, t=t: h.reciprocal(out=st_rstd[:, t:t + 1], in_=lstd[:, :]), inc="dve")
            vsync()
            do("vector", lambda h, t=t: h.tensor_scalar(out=st_nmr[:, t:t + 1], in0=lm[:, :],
                                                        scalar1=st_rstd[:, t:t + 1], scalar2=-1.0,
                                                        op0=ALU.mult, op1=ALU.mult), inc="dve")
            mark[("rstd", t)] = counts["dve"]
            do("scalar", lambda h, t=t, p=p: h.activation(xnf[p][:, :], xt[p][:, :], AF.Identity,
                                                          bias=st_nmr[:, t:t + 1], scale=st_rstd[:, t:t + 1]),
               waits=[("dve", mark[("rstd", t)]), ("dve", mark.get(("u", t - 2), 0))],
               inc="sca")
            mark[("xn", t)] = counts["sca"]
            wait("vector", "sca", mark[("xn", t)])
            wait("vector", "gp", mark.get(("gpxn", t - 2), 0))
            do("vector", lambda h, p=p: h.tensor_copy(out=xnb[p][:, :], in_=xnf[p][:, :]), inc="dve")
            mark[("xnb", t)] = counts["dve"]
            wait("vector", "gp", mark.get(("gpxq", t - 2), 0))
            emit_quant(ph["qs"], lambda p=p: xnf[p][:, :], lambda p=p: xqb[p][:, :], E, inc="dve")
            mark[("q", t)] = counts["dve"]
            mark[("u", t)] = counts["dve"]
            do("gpsimd", lambda h, t=t, p=p: h.dma_start(out=d_xn[t * 128:(t + 1) * 128, :], in_=xnb[p][:, :]),
               waits=[("dve", mark[("xnb", t)])], inc="gp", n=16)
            mark[("gpxn", t)] = counts["gp"]
            do("gpsimd", lambda h, t=t, p=p: h.dma_start(out=d_xq[t * 128:(t + 1) * 128, :], in_=xqb[p][:, :]),
               waits=[("dve", mark[("q", t)])], inc="gp", n=16)
            mark[("gpxq", t)] = counts["gp"]
        if d_dbg is not None:
            do("gpsimd", lambda h: h.dma_start(out=d_dbg[:, 0:TT], in_=st_rstd[:, :]),
               waits=[("dve", counts["dve"])], inc="gp", n=16)
            do("gpsimd", lambda h: h.dma_start(out=d_dbg[:, TT:2 * TT], in_=st_nmr[:, :]),
               inc="gp", n=16)
        mark["p0_gp"] = counts["gp"]

    def phase0b(ph):
        wstage = ph["wstage"]; wqb = ph["wqb"]
        units = []
        for w_src, w_dst in ((d_wg, d_wgq), (d_wu, d_wuq)):
            for r in range(22):
                units.append((w_src, w_dst, r, [(0, E)]))
        for r in range(8):
            units.append((d_wd, d_wdq, r, [(0, IH), (IH, IH)]))
        qcount = 0
        for ui, (w_src, w_dst, r, parts) in enumerate(units):
            p = ui % 2
            ncols = w_src.shape[1]
            do("sync", lambda h, w=w_src, r=r, p=p, ncols=ncols:
               h.dma_start(out=wstage[p][:, 0:ncols], in_=w[r * 128:(r + 1) * 128, :]),
               inc="ld", n=16,
               waits=[("dve", mark.get(("wq_done", ui - 2), 0))])
            mark[("ldw", ui)] = counts["ld"]
            wait("vector", "ld", mark[("ldw", ui)])
            for off, F in parts:
                qp2 = qcount % 2
                qcount += 1
                wait("vector", "gp", mark.get(("gpw", qcount - 3), 0))
                emit_quant(ph["qs"], lambda p=p, off=off, F=F: wstage[p][:, off:off + F],
                           lambda qp2=qp2, F=F: wqb[qp2][:, 0:F], F, inc="dve")
                mark[("wqd", qcount - 1)] = counts["dve"]
                do("gpsimd", lambda h, w=w_dst, r=r, off=off, F=F, qp2=qp2:
                   h.dma_start(out=w[r * 128:(r + 1) * 128, off:off + F], in_=wqb[qp2][:, 0:F]),
                   waits=[("dve", mark[("wqd", qcount - 1)])], inc="gp", n=16)
                mark[("gpw", qcount - 1)] = counts["gp"]
            mark[("wq_done", ui)] = counts["dve"]
        mark["p0b_gp"] = counts["gp"]

    def phase1(ph):
        xnT = ph["xnT"]; wqb = ph["wqb"]; wkb = ph["wkb"]; wvb = ph["wvb"]
        wvup = ph["wvup"]; vdn = ph["vdn"]; kp = ph["kp"]; qp = ph["qp"]
        junk128 = ph["junk128"]; junkexp = ph["junkexp"]
        sums = ph["sums"]; rsall = ph["rsall"]; diag = ph["diag"]
        ed = ph["ed"]; rr = ph["rr"]; da = ph["da"]
        daT = ph["daT"]; daflat = ph["daflat"]
        p_pa, p_pb, p_sa, p_sb = ph["p_pa"], ph["p_pb"], ph["p_sa"], ph["p_sb"]

        do("vector", lambda h: h.memset(sums[:, :], 0.0), inc="dve")

        wait("sync", "gp", mark["p0_gp"])
        for u in range(EC):
            do("sync", lambda h, u=u: h.dma_start(out=xnT[:, u * S:(u + 1) * S],
                                                  in_=d_xn[:, u * 128:(u + 1) * 128], transpose=True),
               inc="ld", n=16)
        mark["xnT"] = counts["ld"]
        wvflat = d_wvup.rearrange("n d e -> (n d) e")
        for k in range(8):
            do("gpsimd", lambda h, k=k: h.dma_start(out=wvup[:, k * E:(k + 1) * E],
                                                    in_=wvflat[k * 128:(k + 1) * 128, :]),
               inc="gp", n=16)
        mark["wvup"] = counts["gp"]

        def load_pair_w(c):
            par = c % 2
            for t in (0, 1):
                for wsrc, wdst in ((d_wq, wqb), (d_wk, wkb), (d_wv, wvb)):
                    do("gpsimd", lambda h, wsrc=wsrc, wdst=wdst, c=c, t=t, par=par:
                       h.dma_start(
                           out=wdst[par][:, :].rearrange("p (u t d) -> p u t d", t=2, d=D)[:, :, t, :],
                           in_=wsrc[2 * c + t, :, :].rearrange("(u p) d -> p u d", p=128)),
                       waits=[("pe", mark.get(("prodend", c - 2), 0))],
                       inc="gp", n=16)
            mark[("wpair", c)] = counts["gp"]

        load_pair_w(0)
        su = 0
        pp = 0
        for c in range(PAIRS):
            par = c % 2
            if c + 1 < PAIRS:
                load_pair_w(c + 1)
            wait("tensor", "gp", mark[("wpair", c)])
            wait("tensor", "ld", mark["xnT"])
            wait("tensor", "dve", mark.get(("scend_dve", c - 1), 0))
            for kind, wb, dest in (("q", wqb, None), ("k", wkb, None), ("v", wvb, None)):
                for j in range(8):
                    ps = (p_pa, p_pb)[pp % 2]
                    pp += 1
                    wait("tensor", "dve", mark.get(("evac", pp - 3), 0))
                    for u in range(EC):
                        do("tensor", lambda h, wb=wb, par=par, u=u, j=j, ps=ps:
                           h.matmul(ps[:, :], wb[par][:, u * 128:(u + 1) * 128],
                                    xnT[:, u * S + j * 512: u * S + (j + 1) * 512],
                                    start=(u == 0), stop=(u == EC - 1)),
                           inc=("pe" if u == EC - 1 else None))
                    mark[("prod", c, kind, j)] = counts["pe"]
                    wait("vector", "pe", mark[("prod", c, kind, j)])
                    if kind == "q":
                        do("vector", lambda h, ps=ps, j=j: h.tensor_scalar(
                            out=qp[:, j * 512:(j + 1) * 512], in0=ps[:, :],
                            scalar1=0.125, scalar2=None, op0=ALU.mult), inc="dve")
                    elif kind == "k":
                        do("vector", lambda h, ps=ps, j=j: h.tensor_copy(
                            out=kp[:, j * 512:(j + 1) * 512], in_=ps[:, :]), inc="dve")
                    else:
                        do("vector", lambda h, ps=ps, c=c, j=j: h.tensor_copy(
                            out=vdn[:, c * S + j * 512: c * S + (j + 1) * 512], in_=ps[:, :]), inc="dve")
                    mark[("evac", pp - 1)] = counts["dve"]
            mark[("prodend", c)] = counts["pe"]
            wait("tensor", "dve", mark[("evac", pp - 1)])
            for t in (0, 1):
                h_idx = 2 * c + t
                lo, hi = t * 64, (t + 1) * 64
                wait("scalar", "dve", mark.get(("hred", h_idx - 1), 0))
                do("vector", lambda h: h.memset(sums[:, :], 0.0),
                   waits=[("sca", mark.get(("hexp", h_idx - 1), 0))], inc="dve")
                mark[("hclr", h_idx)] = counts["dve"]
                wait("scalar", "dve", mark[("hclr", h_idx)])
                for r in range(TT):
                    s0 = r * 128
                    ustart = s0
                    ui = 0
                    while ustart < S:
                        ulen = min(1536, S - ustart)
                        ps = (p_sa, p_sb)[su % 2]
                        su += 1
                        wait("tensor", "sca", mark.get(("exp", su - 3), 0))
                        cs = ustart
                        while cs < ustart + ulen:
                            cl = 128 if (ui == 0 and cs == s0) else min(512, ustart + ulen - cs)
                            last = (cs + cl) >= (ustart + ulen)
                            do("tensor", lambda h, lo=lo, hi=hi, s0=s0, cs=cs, cl=cl, ps=ps, ustart=ustart:
                               h.matmul(ps[:, cs - ustart: cs - ustart + cl],
                                        kp[lo:hi, s0:s0 + 128], qp[lo:hi, cs:cs + cl],
                                        start=True, stop=True),
                               inc=("pe" if last else None))
                            cs += cl
                        mark[("smm", su - 1)] = counts["pe"]
                        if ui == 0:
                            wait("vector", "pe", mark[("smm", su - 1)])
                            do("vector", lambda h, ps=ps: h.tensor_tensor(
                                out=ps[:, 0:128], in0=ps[:, 0:128], in1=c_tri[:, :], op=ALU.add))
                            do("vector", lambda h, ps=ps, h_idx=h_idx, r=r: h.scalar_tensor_tensor(
                                out=junk128[:, :], in0=ps[:, 0:128], scalar=1.0,
                                in1=c_id01[:, :], op0=ALU.mult, op1=ALU.mult,
                                accum_out=diag[:, h_idx * 32 + r: h_idx * 32 + r + 1]), inc="dve")
                            mark[("msk", su - 1)] = counts["dve"]
                            wait("scalar", "dve", mark[("msk", su - 1)])
                        else:
                            wait("scalar", "pe", mark[("smm", su - 1)])
                        do("scalar", lambda h, ps=ps, ulen=ulen, r=r, ui=ui:
                           h.activation(junkexp[:, 0:ulen], ps[:, 0:ulen], AF.Exp, scale=1.0,
                                        accum_out=sums[:, r * 3 + ui: r * 3 + ui + 1]),
                           inc="sca")
                        mark[("exp", su - 1)] = counts["sca"]
                        ustart += ulen
                        ui += 1
                mark[("hexp", h_idx)] = counts["sca"]
                do("vector", lambda h, h_idx=h_idx: h.tensor_reduce(
                    out=rsall[:, h_idx * 32:(h_idx + 1) * 32],
                    in_=sums[:, :].rearrange("p (r u) -> p r u", u=3),
                    axis=AX.X, op=ALU.add),
                   waits=[("sca", mark[("hexp", h_idx)])], inc="dve")
                vsync()
                mark[("hred", h_idx)] = counts["dve"]
            mark[("scend", c)] = counts["pe"]
            mark[("scend_dve", c)] = counts["dve"]
        mark["allexp"] = counts["sca"]

        for hh in range(NH):
            c, t = hh // 2, hh % 2
            lo, hi = t * 64, (t + 1) * 64
            hp = hh % 2
            psT = (p_sa, p_sb)[hp]
            psU = (p_sb, p_sa)[hp]
            wait("vector", "sca", mark["allexp"])
            do("vector", lambda h, hh=hh: h.reciprocal(
                out=rr[:, :], in_=rsall[:, hh * 32:(hh + 1) * 32]), inc="dve")
            vsync()
            mark[("rr", hh)] = counts["dve"]
            do("scalar", lambda h, hh=hh: h.activation(ed[:, :], diag[:, hh * 32:(hh + 1) * 32], AF.Exp),
               waits=[("dve", mark.get(("da", hh - 1), 0))], inc="sca")
            mark[("ed", hh)] = counts["sca"]
            do("vector", lambda h: h.tensor_tensor(out=da[:, :], in0=ed[:, :], in1=rr[:, :], op=ALU.mult),
               waits=[("sca", mark[("ed", hh)])], inc="dve")
            mark[("da", hh)] = counts["dve"]
            do("tensor", lambda h, psT=psT: h.transpose(psT[0:32, 0:128], da[:, :], c_idf[:, :]),
               waits=[("dve", mark[("da", hh)]), ("sca", mark["allexp"]),
                      ("dve", mark.get(("vdnlast", hh - 1), 0))],
               inc="pe")
            mark[("tp", hh)] = counts["pe"]
            do("vector", lambda h, psT=psT: h.tensor_copy(out=daT[:, :], in_=psT[0:32, 0:128]),
               waits=[("pe", mark[("tp", hh)]), ("gp", mark.get(("datw", hh - 1), 0))], inc="dve")
            mark[("daT", hh)] = counts["dve"]
            do("gpsimd", lambda h, hh=hh: h.dma_start(
                out=d_dat[hh:hh + 1, :].rearrange("a (r f) -> (a r) f", f=128), in_=daT[:, :]),
               waits=[("dve", mark[("daT", hh)])], inc="gp", n=16)
            mark[("datw", hh)] = counts["gp"]
            do("gpsimd", lambda h, hh=hh, hp=hp: h.dma_start(
                out=daflat[hp][:, :], in_=d_dat[hh:hh + 1, :]),
               waits=[("gp", mark[("datw", hh)]), ("pe", mark.get(("bcend", hh - 2), 0))],
               inc="gp", n=16)
            mark[("datr", hh)] = counts["gp"]
            wait("tensor", "gp", mark[("datr", hh)])
            for j in range(8):
                sl = (j % 2) * 512
                do("tensor", lambda h, psU=psU, lo=lo, hi=hi, sl=sl, hp=hp, j=j:
                   h.matmul(psU[lo:hi, sl:sl + 512], c_ones[0:1, 0:64],
                            daflat[hp][0:1, j * 512:(j + 1) * 512], start=True, stop=True),
                   waits=[("dve", mark.get(("vdn", hh, j - 2), 0))], inc="pe")
                mark[("bc", hh, j)] = counts["pe"]
                do("vector", lambda h, psU=psU, lo=lo, hi=hi, sl=sl, c=c, j=j:
                   h.tensor_tensor(out=vdn[lo:hi, c * S + j * 512: c * S + (j + 1) * 512],
                                   in0=vdn[lo:hi, c * S + j * 512: c * S + (j + 1) * 512],
                                   in1=psU[lo:hi, sl:sl + 512], op=ALU.mult),
                   waits=[("pe", mark[("bc", hh, j)])], inc="dve")
                mark[("vdn", hh, j)] = counts["dve"]
            mark[("bcend", hh)] = counts["pe"]
            mark[("vdnlast", hh)] = counts["dve"]
        mark["tails_dve"] = counts["dve"]

    def phase1c(ph):
        xt2 = ph["xt2"]; xn2 = ph["xn2"]; sbf = ph["sbf"]
        vdn = ph["vdn"]; wvup = ph["wvup"]; p_d = ph["p_d"]
        for tile in range(TT):
            par = tile % 2
            do("sync", lambda h, tile=tile, par=par: h.dma_start(
                out=xt2[par][:, :], in_=d_x[tile * 128:(tile + 1) * 128, :]),
               waits=[("sca", mark.get(("xn2", tile - 2), 0))], inc="ld", n=16)
            mark[("ldx2", tile)] = counts["ld"]
            wait("tensor", "dve", mark["tails_dve"])
            wait("tensor", "gp", mark["wvup"])
            wait("tensor", "dve", mark.get(("sst", tile - 2), 0))
            for k in range(8):
                for half in (0, 1):
                    do("tensor", lambda h, k=k, half=half, tile=tile, par=par:
                       h.matmul(p_d[par][:, half * 512:(half + 1) * 512],
                                vdn[:, k * S + tile * 128: k * S + (tile + 1) * 128],
                                wvup[:, k * E + half * 512: k * E + (half + 1) * 512],
                                start=(k == 0), stop=(k == 7)),
                       inc=("pe" if (k == 7 and half == 1) else None))
            mark[("dmm", tile)] = counts["pe"]
            do("scalar", lambda h, tile=tile, par=par: h.activation(
                xn2[par][:, :], xt2[par][:, :], AF.Identity,
                bias=st_nmr[:, tile:tile + 1], scale=st_rstd[:, tile:tile + 1]),
               waits=[("ld", mark[("ldx2", tile)]), ("dve", mark.get(("sst", tile - 2), 0))],
               inc="sca")
            mark[("xn2", tile)] = counts["sca"]
            do("vector", lambda h, par=par: h.scalar_tensor_tensor(
                out=sbf[par][:, :], in0=p_d[par][:, :], scalar=1.0 / 16.0,
                in1=xn2[par][:, :], op0=ALU.mult, op1=ALU.add),
               waits=[("pe", mark[("dmm", tile)]), ("sca", mark[("xn2", tile)]),
                      ("gp", mark.get(("gps", tile - 2), 0))],
               inc="dve")
            mark[("sst", tile)] = counts["dve"]
            do("gpsimd", lambda h, tile=tile, par=par: h.dma_start(
                out=d_s[tile * 128:(tile + 1) * 128, :], in_=sbf[par][:, :]),
               waits=[("dve", mark[("sst", tile)])], inc="gp", n=16)
            mark[("gps", tile)] = counts["gp"]

    def phase2(ph):
        wgT = ph["wgT"]; wuT = ph["wuT"]; wdT = ph["wdT"]
        xqg = ph["xqg"]; hqT = ph["hqT"]; sg = ph["sg"]; hqt = ph["hqt"]
        gact = ph["gact"]; xt3 = ph["xt3"]; sbt = ph["sbt"]
        p_g, p_u, p_dn = ph["p_g"], ph["p_u"], ph["p_dn"]

        wait("sync", "gp", mark["p0b_gp"])
        for k in range(22):
            do("sync", lambda h, k=k: h.dma_start(out=wdT[:, k * E:(k + 1) * E],
                                                  in_=d_wdq[:, k * 128:(k + 1) * 128], transpose=True),
               inc="ld", n=16)
        mark["wdT"] = counts["ld"]

        for qq in range(4):
            wait("sync", "pe", mark.get(("gu_last", qq - 1), 0))
            for u in range(EC):
                do("sync", lambda h, u=u, qq=qq: h.dma_start(
                    out=wgT[:, u * IQ:(u + 1) * IQ],
                    in_=d_wgq[qq * IQ:(qq + 1) * IQ, u * 128:(u + 1) * 128], transpose=True),
                   inc="ld", n=16)
                do("sync", lambda h, u=u, qq=qq: h.dma_start(
                    out=wuT[:, u * IQ:(u + 1) * IQ],
                    in_=d_wuq[qq * IQ:(qq + 1) * IQ, u * 128:(u + 1) * 128], transpose=True),
                   inc="ld", n=16)
            mark[("wh", qq)] = counts["ld"]
            for g in range(4):
                wait("sync", "pe", mark.get(("gug", g - 1, qq), mark.get(("gug", 3, qq - 1), 0)))
                for u in range(EC):
                    do("sync", lambda h, u=u, g=g: h.dma_start(
                        out=xqg[:, u * 1024:(u + 1) * 1024],
                        in_=d_xq[g * 1024:(g + 1) * 1024, u * 128:(u + 1) * 128], transpose=True),
                       inc="ld", n=16)
                mark[("xqg", g, qq)] = counts["ld"]
                for tl in range(8):
                    gt = g * 8 + tl
                    par2 = gt % 2
                    wait("tensor", "ld", mark[("wh", qq)])
                    wait("tensor", "ld", mark[("xqg", g, qq)])
                    wait("tensor", "sca", mark.get(("silu", gt - 1, qq), mark.get(("silu", 31, qq - 1), 0)))
                    wait("tensor", "dve", mark.get(("hmul", gt - 1, qq), mark.get(("hmul", 31, qq - 1), 0)))
                    for u in range(EC):
                        for ci, (co, cl) in enumerate(((0, 512), (512, 192))):
                            do("tensor", lambda h, u=u, co=co, cl=cl, tl=tl:
                               h.matmul(p_g[:, co:co + cl],
                                        xqg[:, u * 1024 + tl * 128: u * 1024 + (tl + 1) * 128],
                                        wgT[:, u * IQ + co: u * IQ + co + cl],
                                        start=(u == 0), stop=(u == EC - 1)))
                            do("tensor", lambda h, u=u, co=co, cl=cl, tl=tl:
                               h.matmul(p_u[:, co:co + cl],
                                        xqg[:, u * 1024 + tl * 128: u * 1024 + (tl + 1) * 128],
                                        wuT[:, u * IQ + co: u * IQ + co + cl],
                                        start=(u == 0), stop=(u == EC - 1)),
                               inc=("pe" if (u == EC - 1 and ci == 1) else None))
                    mark[("gu", gt, qq)] = counts["pe"]
                    do("scalar", lambda h, par2=par2: h.activation(sg[par2][:, :], p_g[:, 0:IQ], AF.Silu),
                       waits=[("pe", mark[("gu", gt, qq)]), ("dve", mark.get(("hmul", gt - 2, qq), 0))],
                       inc="sca")
                    mark[("silu", gt, qq)] = counts["sca"]
                    do("vector", lambda h, par2=par2, qs=ph["qs"]: h.scalar_tensor_tensor(
                        out=qs["q1"][:, 0:IQ], in0=sg[par2][:, :], scalar=1.0,
                        in1=p_u[:, 0:IQ], op0=ALU.mult, op1=ALU.mult),
                       waits=[("sca", mark[("silu", gt, qq)])], inc="dve")
                    mark[("hmul", gt, qq)] = counts["dve"]
                    wait("vector", "gp", mark.get(("gphq", gt - 2, qq), 0))
                    emit_quant(ph["qs"], lambda qs=ph["qs"]: qs["q1"][:, 0:IQ],
                               lambda par2=par2: hqt[par2][:, :], IQ, inc="dve")
                    mark[("hq", gt, qq)] = counts["dve"]
                    do("gpsimd", lambda h, gt=gt, qq=qq, par2=par2: h.dma_start(
                        out=d_hq[gt * 128:(gt + 1) * 128, qq * IQ:(qq + 1) * IQ], in_=hqt[par2][:, :]),
                       waits=[("dve", mark[("hq", gt, qq)])], inc="gp", n=16)
                    mark[("gphq", gt, qq)] = counts["gp"]
                mark[("gug", g, qq)] = counts["pe"]
                if qq == 3:
                    wait("sync", "gp", mark[("gphq", g * 8 + 7, 3)])
                    wait("sync", "pe", mark.get(("dnend", g - 1), 0))
                    for k in range(22):
                        do("sync", lambda h, k=k, g=g: h.dma_start(
                            out=hqT[:, k * 1024:(k + 1) * 1024],
                            in_=d_hq[g * 1024:(g + 1) * 1024, k * 128:(k + 1) * 128], transpose=True),
                           inc="ld", n=16)
                    mark[("hqT", g)] = counts["ld"]
                    for tl in range(8):
                        gt = g * 8 + tl
                        par = tl % 2
                        wait("tensor", "ld", mark[("hqT", g)])
                        wait("tensor", "ld", mark["wdT"])
                        wait("tensor", "sca", mark.get(("gact", gt - 2), 0))
                        for k in range(22):
                            for half in (0, 1):
                                do("tensor", lambda h, k=k, half=half, tl=tl, par=par:
                                   h.matmul(p_dn[par][:, half * 512:(half + 1) * 512],
                                            hqT[:, k * 1024 + tl * 128: k * 1024 + (tl + 1) * 128],
                                            wdT[:, k * E + half * 512: k * E + (half + 1) * 512],
                                            start=(k == 0), stop=(k == 21)),
                                   inc=("pe" if (k == 21 and half == 1) else None))
                        mark[("dn", gt)] = counts["pe"]
                        do("scalar", lambda h, par=par: h.activation(gact[par][:, :], p_dn[par][:, 0:E], AF.Silu),
                           waits=[("pe", mark[("dn", gt)]), ("dve", mark.get(("out", gt - 2), 0))],
                           inc="sca")
                        mark[("gact", gt)] = counts["sca"]
                        do("sync", lambda h, gt=gt, par=par: h.dma_start(
                            out=xt3[par][:, :], in_=d_x[gt * 128:(gt + 1) * 128, :]),
                           waits=[("dve", mark.get(("out", gt - 2), 0)),
                                  ("gp", mark.get(("gpy", gt - 2), 0))], inc="ld", n=16)
                        do("sync", lambda h, gt=gt, par=par: h.dma_start(
                            out=sbt[par][:, :], in_=d_s[gt * 128:(gt + 1) * 128, :]),
                           waits=[("gp", mark[("gps", gt)]),
                                  ("dve", mark.get(("out", gt - 2), 0))], inc="ld", n=16)
                        mark[("xs3", gt)] = counts["ld"]
                        do("vector", lambda h, par=par: h.tensor_tensor(
                            out=gact[par][:, :], in0=sbt[par][:, :], in1=gact[par][:, :], op=ALU.mult),
                           waits=[("sca", mark[("gact", gt)]), ("ld", mark[("xs3", gt)])])
                        do("vector", lambda h, par=par: h.tensor_tensor(
                            out=xt3[par][:, :], in0=xt3[par][:, :], in1=gact[par][:, :], op=ALU.add),
                           inc="dve")
                        mark[("out", gt)] = counts["dve"]
                        do("gpsimd", lambda h, gt=gt, par=par: h.dma_start(
                            out=d_y[gt * 128:(gt + 1) * 128, :], in_=xt3[par][:, :]),
                           waits=[("dve", mark[("out", gt)])], inc="gp", n=16)
                        mark[("gpy", gt)] = counts["gp"]
                    mark[("dnend", g)] = counts["pe"]
            mark[("gu_last", qq)] = counts["pe"]
        do("gpsimd", lambda h: h.engine_nop(), waits=[("gp", counts["gp"])])

    def run_phase(walk, allocs):
        snap = dict(counts)
        snap_mark = dict(mark)
        with nc.Block() as block:
            for eng_name in ENGINES:
                counts.clear(); counts.update(snap)
                mark.clear(); mark.update(snap_mark)
                Ctx.cur = eng_name

                def cb(engine, walk=walk, allocs=allocs):
                    Ctx.eng = engine
                    walk(allocs)
                getattr(block, eng_name)(cb)
        Ctx.cur = None
        Ctx.eng = None

    with ExitStack() as ps:
        al = {}
        al["xt"] = [ps.enter_context(nc.sbuf_tensor(f"xt{i}", [128, E], F32)) for i in range(2)]
        al["xnf"] = [ps.enter_context(nc.sbuf_tensor(f"xnf{i}", [128, E], F32)) for i in range(2)]
        al["xnb"] = [ps.enter_context(nc.sbuf_tensor(f"xnb{i}", [128, E], BF16)) for i in range(2)]
        al["xqb"] = [ps.enter_context(nc.sbuf_tensor(f"xqb{i}", [128, E], BF16)) for i in range(2)]
        al["qs"] = alloc_quant(ps, E, "a")

        def w0(allocs):
            phase_init()
            phase0(allocs)
            barrier()
        run_phase(w0, al)

    with ExitStack() as ps:
        al = {}
        al["wstage"] = [ps.enter_context(nc.sbuf_tensor(f"wst{i}", [128, I], F32)) for i in range(2)]
        al["wqb"] = [ps.enter_context(nc.sbuf_tensor(f"wqb{i}", [128, IH], BF16)) for i in range(2)]
        al["qs"] = alloc_quant(ps, IH, "b")

        def w0b(allocs):
            phase0b(allocs)
            barrier()
        run_phase(w0b, al)

    with ExitStack() as pso:
        vdn_t = pso.enter_context(nc.sbuf_tensor("vdn", [128, 8 * S], BF16))
        wvup_t = pso.enter_context(nc.sbuf_tensor("wvup_sb", [128, 8 * E], BF16))
        with ExitStack() as ps:
            al = {}
            al["vdn"] = vdn_t
            al["wvup"] = wvup_t
            al["xnT"] = ps.enter_context(nc.sbuf_tensor("xnT", [128, EC * S], BF16))
            al["wqb"] = [ps.enter_context(nc.sbuf_tensor(f"p1wq{i}", [128, 1024], BF16)) for i in range(2)]
            al["wkb"] = [ps.enter_context(nc.sbuf_tensor(f"p1wk{i}", [128, 1024], BF16)) for i in range(2)]
            al["wvb"] = [ps.enter_context(nc.sbuf_tensor(f"p1wv{i}", [128, 1024], BF16)) for i in range(2)]
            al["kp"] = ps.enter_context(nc.sbuf_tensor("kp", [128, S], BF16))
            al["qp"] = ps.enter_context(nc.sbuf_tensor("qp", [128, S], BF16))
            al["junk128"] = ps.enter_context(nc.sbuf_tensor("junk128", [128, 128], F32))
            al["junkexp"] = ps.enter_context(nc.sbuf_tensor("junkexp", [128, 1536], BF16))
            al["sums"] = ps.enter_context(nc.sbuf_tensor("sums", [128, 96], F32))
            al["rsall"] = ps.enter_context(nc.sbuf_tensor("rsall", [128, NH * 32], F32))
            al["diag"] = ps.enter_context(nc.sbuf_tensor("diag", [128, NH * 32], F32))
            al["ed"] = ps.enter_context(nc.sbuf_tensor("ed", [128, 32], F32))
            al["rr"] = ps.enter_context(nc.sbuf_tensor("rr", [128, 32], F32))
            al["da"] = ps.enter_context(nc.sbuf_tensor("da", [128, 32], F32))
            al["daT"] = ps.enter_context(nc.sbuf_tensor("daT", [32, 128], BF16))
            al["daflat"] = [ps.enter_context(nc.sbuf_tensor(f"daflat{i}", [1, S], BF16)) for i in range(2)]
            al["p_pa"] = ps.enter_context(nc.psum_tensor("p_pa", [128, 512], F32))
            al["p_pb"] = ps.enter_context(nc.psum_tensor("p_pb", [128, 512], F32))
            al["p_sa"] = ps.enter_context(nc.psum_tensor("p_sa", [128, 1536], F32))
            al["p_sb"] = ps.enter_context(nc.psum_tensor("p_sb", [128, 1536], F32))

            def w1(allocs):
                phase1(allocs)
                barrier()
            run_phase(w1, al)

        with ExitStack() as ps2:
            al1c = {"vdn": vdn_t, "wvup": wvup_t}
            al1c["xt2"] = [ps2.enter_context(nc.sbuf_tensor(f"xt2_{i}", [128, E], F32)) for i in range(2)]
            al1c["xn2"] = [ps2.enter_context(nc.sbuf_tensor(f"xn2_{i}", [128, E], F32)) for i in range(2)]
            al1c["sbf"] = [ps2.enter_context(nc.sbuf_tensor(f"sbf{i}", [128, E], BF16)) for i in range(2)]
            al1c["p_d"] = [ps2.enter_context(nc.psum_tensor(f"p_d{i}", [128, E], F32)) for i in range(2)]

            def w1c(allocs):
                phase1c(allocs)
                barrier()
            run_phase(w1c, al1c)

    with ExitStack() as ps:
        al = {}
        al["wgT"] = ps.enter_context(nc.sbuf_tensor("wgT", [128, EC * IQ], BF16))
        al["wuT"] = ps.enter_context(nc.sbuf_tensor("wuT", [128, EC * IQ], BF16))
        al["wdT"] = ps.enter_context(nc.sbuf_tensor("wdT", [128, 22 * E], BF16))
        al["xqg"] = ps.enter_context(nc.sbuf_tensor("xqg", [128, EC * 1024], BF16))
        al["hqT"] = ps.enter_context(nc.sbuf_tensor("hqT", [128, 22 * 1024], BF16))
        al["sg"] = [ps.enter_context(nc.sbuf_tensor(f"sg{i}", [128, IQ], F32)) for i in range(2)]
        al["hqt"] = [ps.enter_context(nc.sbuf_tensor(f"hqt{i}", [128, IQ], BF16)) for i in range(2)]
        al["gact"] = [ps.enter_context(nc.sbuf_tensor(f"gact{i}", [128, E], F32)) for i in range(2)]
        al["xt3"] = [ps.enter_context(nc.sbuf_tensor(f"xt3_{i}", [128, E], F32)) for i in range(2)]
        al["sbt"] = [ps.enter_context(nc.sbuf_tensor(f"sbt{i}", [128, E], BF16)) for i in range(2)]
        al["qs"] = alloc_quant(ps, IQ, "c")
        al["p_g"] = ps.enter_context(nc.psum_tensor("p_g", [128, IQ], F32))
        al["p_u"] = ps.enter_context(nc.psum_tensor("p_u", [128, IQ], F32))
        al["p_dn"] = [ps.enter_context(nc.psum_tensor(f"p_dn{i}", [128, E], F32)) for i in range(2)]

        def w2(allocs):
            phase2(allocs)
        run_phase(w2, al)

    semstack.close()
    ges.close()
    return nc


def _consts():
    f = np.arange(128, dtype=np.int64)
    tri = np.where(f[None, :] >= f[:, None], np.float32(0.0), np.float32(-1e30)).astype(np.float32)
    id01 = np.eye(128, dtype=np.float32)
    idf = np.eye(128, dtype=np.float32)
    ones = np.ones((1, 64), dtype=ml_dtypes.bfloat16)
    c3ff = np.full((128, 1), 0x1FF, dtype=np.int32)
    return {"trimask": tri, "ident01": id01, "identf": idf, "onesbf": ones, "c3ff": c3ff}


def kernel(L0_attn_inputs, L0_attn_WQ, L0_attn_WK, L0_attn_WV_down, L0_attn_WV_up,
           L0_ffn_w_gate, L0_ffn_w_up, L0_ffn_w_down):
    x = np.ascontiguousarray(np.asarray(L0_attn_inputs))
    w = {
        "wq": np.ascontiguousarray(np.asarray(L0_attn_WQ)),
        "wk": np.ascontiguousarray(np.asarray(L0_attn_WK)),
        "wv": np.ascontiguousarray(np.asarray(L0_attn_WV_down)),
        "wvup": np.ascontiguousarray(np.asarray(L0_attn_WV_up)),
        "wg": np.ascontiguousarray(np.asarray(L0_ffn_w_gate)),
        "wu": np.ascontiguousarray(np.asarray(L0_ffn_w_up)),
        "wd": np.ascontiguousarray(np.asarray(L0_ffn_w_down)),
    }
    consts = _consts()
    nc = build_program(dbg=False)
    in_maps = []
    for core in range(8):
        m = {"x": np.ascontiguousarray(x[core])}
        m.update(w)
        m.update(consts)
        in_maps.append(m)
    res = run_bass_kernel_spmd(nc, in_maps, core_ids=list(range(8)))
    out = np.stack([np.asarray(r["y"]) for r in res.results], axis=0)
    return out.astype(np.float32)


if __name__ == "__main__":
    nc = build_program(dbg=True)
    n = sum(1 for _ in nc.all_instructions())
    print(f"built ok, {n} instructions")
